# version 08 (final)
# speedup vs baseline: 1.7706x; 1.0996x over previous
import numpy as np
import ml_dtypes
from contextlib import ExitStack

import concourse.bass as bass
import concourse.mybir as mybir
from concourse.bass_utils import run_bass_kernel_spmd
from concourse.bass_core_rust import IndirectOffsetOnAxis

F32 = mybir.dt.float32
BF16 = mybir.dt.bfloat16
F16 = mybir.dt.float16
U32 = mybir.dt.uint32
AF = mybir.ActivationFunctionType
ALU = mybir.AluOpType
AX = mybir.AxisListType

S = 4096
D = 1024
NH = 16
HD = 64
NPAIR = 8
STILES = 32
DTILES = 8
QCH = 8
KTILES = 32
E = 8
II = 4096
ITILES = 32
IBLK = 4
SEL = 640
SELT = 5
NBISECT = 46
EPS = 1e-5
ISQ = 0.125

PE, ACT, DVE, POOL, SP = "pe", "act", "dve", "pool", "sp"


class Sched:

    def __init__(self):
        self.ops = {q: [] for q in (PE, ACT, DVE, POOL, SP)}
        self.count = {q: 0 for q in self.ops}
        self.lastw = {}

    def emit(self, q, fn, deps=(), inc=1):
        waits = []
        all_deps = list(deps)
        if q in (ACT, DVE, POOL) and fn is not None and self.count[q] > 0:
            all_deps.append((q, self.count[q]))
        for (pq, cnt) in all_deps:
            if cnt <= 0:
                continue
            if pq == q and q == PE:
                continue
            key = (q, pq)
            if self.lastw.get(key, -1) < cnt:
                self.lastw[key] = cnt
                waits.append((pq, cnt))
        self.ops[q].append((waits, fn, inc))
        self.count[q] += inc
        return self.count[q]

    def barrier(self):
        snap = dict(self.count)
        for q in self.ops:
            self.emit(q, None, deps=[(p, snap[p]) for p in self.ops if p != q],
                      inc=0)
        return snap


def build_nc(dbg=False, notail=False):
    nc = bass.Bass("TRN2", target_bir_lowering=False)

    ap_x = nc.dram_tensor("x", [S, D], F32, kind="ExternalInput").ap()
    ap_wqT = nc.dram_tensor("wqT", [D, D], F32, kind="ExternalInput").ap()
    ap_wkT = nc.dram_tensor("wkT", [D, D], F32, kind="ExternalInput").ap()
    ap_wvT = nc.dram_tensor("wvT", [D, D], F32, kind="ExternalInput").ap()
    ap_woT = nc.dram_tensor("woT", [D, D], F32, kind="ExternalInput").ap()
    ap_bq2 = nc.dram_tensor("bq2", [128, NPAIR], F32, kind="ExternalInput").ap()
    ap_bk2 = nc.dram_tensor("bk2", [128, NPAIR], F32, kind="ExternalInput").ap()
    ap_bv = nc.dram_tensor("bv", [1, D], F32, kind="ExternalInput").ap()
    ap_bo = nc.dram_tensor("bo", [1, D], F32, kind="ExternalInput").ap()
    ap_relext = nc.dram_tensor("relext", [128, 511], F32, kind="ExternalInput").ap()
    ap_reledge = nc.dram_tensor("reledge", [128, 2], F32, kind="ExternalInput").ap()
    ap_gwT = nc.dram_tensor("gwT", [DTILES, 128, E], F32, kind="ExternalInput").ap()
    ap_wgT = nc.dram_tensor("wgT", [E, D, II], F32, kind="ExternalInput").ap()
    ap_wuT = nc.dram_tensor("wuT", [E, D, II], F32, kind="ExternalInput").ap()
    ap_wdT = nc.dram_tensor("wdT", [E, II, D], F32, kind="ExternalInput").ap()
    ap_hmask = nc.dram_tensor("hmask", [E, S], F32, kind="ExternalInput").ap()
    ap_y = nc.dram_tensor("y", [S, D], F32, kind="ExternalOutput").ap()

    d_xln1 = nc.dram_tensor("d_xln1", [S, D], BF16).ap()
    d_v = nc.dram_tensor("d_v", [NH, S, 65], BF16).ap()
    d_qrp = nc.dram_tensor("d_qrp", [NH, S + 1, 511], BF16).ap()
    d_crow = nc.dram_tensor("d_crow", [NH, S], BF16).ap()
    d_avT = nc.dram_tensor("d_avT", [D, S], BF16).ap()
    d_x1 = nc.dram_tensor("d_x1", [S, D], F32).ap()
    d_xln2 = nc.dram_tensor("d_xln2", [S, D], BF16).ap()
    d_ctmp = nc.dram_tensor("d_ctmp", [E, S], F16).ap()
    d_atmp = nc.dram_tensor("d_atmp", [E, S], F16).ap()
    d_xsel = nc.dram_tensor("d_xsel", [2, SEL, D], BF16).ap()
    d_yg = nc.dram_tensor("d_yg", [E, SEL + 1, D], F32).ap()
    d_rec = nc.dram_tensor("d_rec", [1, 512], F32).ap()
    d_bis = nc.dram_tensor("d_bis", [8, 8], F32).ap()
    d_bist = nc.dram_tensor("d_bist", [8, 64], F32).ap()
    d_aff0 = nc.dram_tensor("d_aff0", [E, S], F32).ap()
    d_msk = nc.dram_tensor("d_msk", [E, S], F32).ap()
    d_csr = nc.dram_tensor("d_csr", [E, S], F32).ap()

    sch = Sched()
    e = sch.emit

    bank_rd = {}

    def rd(key, q, cnt):
        bank_rd.setdefault(key, []).append((q, cnt))
        return cnt

    def wr_deps(key):
        return bank_rd.pop(key, [])

    with ExitStack() as stack:
        def sb(name, shape, dtype):
            return stack.enter_context(nc.sbuf_tensor(name, shape, dtype))

        def ps(name, shape, dtype=F32):
            return stack.enter_context(nc.psum_tensor(name, shape, dtype))

        sem = {q: stack.enter_context(nc.semaphore(f"sem_{q}"))
               for q in (PE, ACT, DVE, POOL, SP)}

        ps_st = [ps(f"ps_st{i}", [128, 1024]) for i in range(2)]
        ps_rot = [ps_st[0][:, 0:512], ps_st[0][:, 512:1024],
                  ps_st[1][:, 0:512], ps_st[1][:, 512:1024]]
        ps_av = [ps(f"ps_av{i}", [128, 512]) for i in range(2)]
        ps_w = ps("ps_w", [128, 1024])

        c_onesr = sb("c_onesr", [1, 128], F32)
        c_onesc = sb("c_onesc", [128, 1], F32)
        c_onesrb = sb("c_onesrb", [1, 128], BF16)
        c_ones2 = sb("c_ones2", [128, 64], F32)
        c_ones2b = sb("c_ones2b", [128, 128], BF16)
        c_onescb = sb("c_onescb", [128, 1], BF16)
        c_iota_f = sb("c_iota_f", [128, 128], F32)
        c_iota_p = sb("c_iota_p", [128, 1], F32)
        sb_bq2 = sb("sb_bq2", [128, NPAIR], F32)
        sb_bk2 = sb("sb_bk2", [128, NPAIR], F32)
        sb_bqs = sb("sb_bqs", [128, NPAIR], F32)
        sb_bv = sb("sb_bv", [1, D], BF16)
        sb_bo = sb("sb_bo", [1, D], BF16)
        sb_relext = sb("sb_relext", [128, 511], BF16)
        sb_reledge = sb("sb_reledge", [128, 2], F32)
        sb_reledgeb = sb("sb_reledgeb", [128, 2], BF16)
        sb_gwT = sb("sb_gwT", [128, DTILES * E], BF16)
        sb_row = sb("sb_row", [1, 1152], F32)
        sb_rowb = sb("sb_rowb", [1, 512], BF16)
        sb_cidx = sb("sb_cidx", [128, E * STILES], U32)
        sb_sidx = sb("sb_sidx", [128, E * SELT], U32)
        sb_gsel = sb("sb_gsel", [128, E * SELT], F32)
        sb_xin = sb("sb_xin", [128, 3 * D], F32)
        sb_sq = sb("sb_sq", [128, D], BF16)
        sb_st = sb("sb_st", [128, 24], F32)
        sb_xln = sb("sb_xln", [128, 2 * D], BF16)
        RM = 1024

        e(POOL, lambda g: g.memset(c_onesr[:, :], 1.0))
        e(POOL, lambda g: g.memset(c_onesc[:, :], 1.0))
        e(POOL, lambda g: g.memset(c_onesrb[:, :], 1.0))
        e(POOL, lambda g: g.memset(c_ones2[:, :], 1.0))
        e(POOL, lambda g: g.memset(c_ones2b[:, :], 1.0))
        e(POOL, lambda g: g.memset(c_onescb[:, :], 1.0))
        e(POOL, lambda g: g.iota(c_iota_f[:, :], pattern=[[1, 128]], base=0,
                                 channel_multiplier=0,
                                 allow_small_or_imprecise_dtypes=True))
        e(POOL, lambda g: g.iota(c_iota_p[:, :], pattern=[[0, 1]], base=0,
                                 channel_multiplier=1,
                                 allow_small_or_imprecise_dtypes=True))
        e(POOL, lambda g: g.dma_start(out=sb_bq2[:, :], in_=ap_bq2), inc=16)
        e(POOL, lambda g: g.dma_start(out=sb_bk2[:, :], in_=ap_bk2), inc=16)
        e(POOL, lambda g: g.dma_start(out=sb_bv[:, :], in_=ap_bv), inc=16)
        e(POOL, lambda g: g.dma_start(out=sb_bo[:, :], in_=ap_bo), inc=16)
        e(POOL, lambda g: g.dma_start(out=sb_relext[:, :], in_=ap_relext), inc=16)
        e(POOL, lambda g: g.dma_start(out=sb_reledge[:, :], in_=ap_reledge), inc=16)
        e(POOL, lambda g: g.dma_start(
            out=sb_gwT[:, :].rearrange("p (t e) -> p t e", t=DTILES),
            in_=ap_gwT.rearrange("t p e -> p t e")), inc=16)
        cpk = sch.count[POOL]
        e(DVE, lambda v: v.tensor_scalar_mul(sb_bqs[:, :], sb_bq2[:, :], ISQ),
          deps=[(POOL, cpk)])
        e(DVE, lambda v: v.tensor_copy(sb_reledgeb[:, :], sb_reledge[:, :]),
          deps=[(POOL, cpk)])
        sch.barrier()

        ln_c4 = {0: 0, 1: 0}

        def layernorm_tile(par, xv, outv, pre_deps, out_deps=()):
            L = par * 12
            st = sb_st
            c1 = e(DVE, lambda v: v.reduce_sum(st[:, L:L + 1], xv, axis=AX.X),
                   deps=pre_deps)
            c2 = e(ACT, lambda a: a.activation(sb_sq[:, :], xv, AF.Square,
                                               accum_out=st[:, L + 1:L + 2]),
                   deps=list(pre_deps) + [(DVE, ln_c4[par])])
            e(DVE, lambda v: v.tensor_scalar_mul(st[:, L + 2:L + 3],
                                                 st[:, L:L + 1], 1.0 / D),
              deps=[(DVE, c1)])
            ln_c4[par] = e(DVE, lambda v: v.tensor_scalar(
                st[:, L + 3:L + 4], st[:, L + 1:L + 2], 1.0 / D,
                EPS, ALU.mult, ALU.add), deps=[(ACT, c2)])
            e(DVE, lambda v: v.tensor_tensor(st[:, L + 4:L + 5],
                                             st[:, L + 2:L + 3],
                                             st[:, L + 2:L + 3], ALU.mult))
            c6 = e(DVE, lambda v: v.tensor_tensor(st[:, L + 5:L + 6],
                                                  st[:, L + 3:L + 4],
                                                  st[:, L + 4:L + 5],
                                                  ALU.subtract))
            c7 = e(ACT, lambda a: a.activation(st[:, L + 6:L + 7],
                                               st[:, L + 5:L + 6], AF.Sqrt),
                    deps=[(DVE, c6)])
            c8 = e(DVE, lambda v: v.reciprocal(st[:, L + 7:L + 8],
                                               st[:, L + 6:L + 7]),
                   deps=[(ACT, c7)])
            c10 = e(DVE, lambda v: v.tensor_scalar(
                outv, xv, st[:, L + 2:L + 3], st[:, L + 7:L + 8],
                ALU.subtract, ALU.mult),
                deps=[(DVE, c8)] + list(out_deps))
            return c10

        stores1 = [0] * STILES
        readers1 = [0] * STILES
        for i in range(STILES):
            xb = i % 3
            xv = sb_xin[:, xb * D:(xb + 1) * D]
            dep_reuse = [(DVE, readers1[i - 3])] if i >= 3 else []
            cl = e(SP, lambda s_, i=i, xv=xv: s_.dma_start(
                out=xv, in_=ap_x[i * 128:(i + 1) * 128, :]),
                deps=dep_reuse, inc=16)
            ob = i % 2
            outv = sb_xln[:, ob * D:(ob + 1) * D]
            pre = [(SP, cl)]
            if i >= 2:
                pre.append((SP, stores1[i - 2]))
            odep = [(SP, stores1[i - 2])] if i >= 2 else []
            c10 = layernorm_tile(i % 2, xv, outv, pre, out_deps=odep)
            readers1[i] = c10
            stores1[i] = e(SP, lambda s_, i=i, outv=outv: s_.dma_start(
                out=d_xln1[i * 128:(i + 1) * 128, :], in_=outv),
                deps=[(DVE, c10)], inc=16)
        sch.barrier()

        with ExitStack() as att:
            def asb(name, shape, dtype):
                return att.enter_context(nc.sbuf_tensor(name, shape, dtype))

            sb_xlnT = asb("sb_xlnT", [128, DTILES * S], BF16)
            sb_wv = asb("sb_wv", [128, DTILES * D], BF16)
            sb_vt = asb("sb_vt", [128, 2 * NH * 65], BF16)
            sb_qta = asb("sb_qta", [128, S], BF16)
            sb_kta = asb("sb_kta", [128, S], BF16)
            sb_klo = asb("sb_klo", [128, S], BF16)
            sb_khi = asb("sb_khi", [128, S], BF16)
            sb_vh = asb("sb_vh", [128, 2 * STILES * 65], BF16)
            sb_sqh = asb("sb_sqh", [128, 1024], BF16)
            sb_qdlo = asb("sb_qdlo", [128, 2 * 512], BF16)
            sb_qdhi = asb("sb_qdhi", [128, 2 * 512], BF16)
            sb_qrp = asb("sb_qrp", [128, 4 * 511], BF16)
            sb_bias = asb("sb_bias", [128, 8 * 128], BF16)
            sb_exp = asb("sb_exp", [128, 2 * 1024], BF16)
            sb_rec = asb("sb_rec", [128, 1024], F32)
            sb_negc = asb("sb_negc", [128, 1], F32)
            sb_avf = asb("sb_avf", [64, 1024], F32)
            sb_avTa = asb("sb_avTa", [64, S], BF16)
            sb_avTb = asb("sb_avTb", [64, S], BF16)

            for dt in range(DTILES):
                for sc in range(8):
                    src = bass.AP(d_xln1.tensor, (sc * 512) * D + dt * 128,
                                  [[D, 512], [1, 128]])
                    e(SP, lambda s_, dt=dt, sc=sc, src=src:
                      s_.dma_start_transpose(
                          sb_xlnT[:, dt * S + sc * 512:dt * S + (sc + 1) * 512],
                          src), inc=16)
            sch.barrier()

            cwv = e(POOL, lambda g: g.dma_start(
                out=sb_wv[:, :].rearrange("p (t c) -> p t c", t=DTILES),
                in_=ap_wvT.rearrange("(t p) c -> p t c", p=128)), inc=16)
            for vb in range(2):
                e(POOL, lambda g, vb=vb: g.memset(
                    sb_vt[:, vb * NH * 65:(vb + 1) * NH * 65].rearrange(
                        "p (h c) -> p h c", h=NH)[:, :, 64:65], 1.0))
            conep = sch.count[POOL]
            vstores = [0] * STILES
            for i in range(STILES):
                mm = 0
                wd = wr_deps(("w",))
                for ncc in range(2):
                    for kp in range(DTILES):
                        mm = e(PE, lambda t, i=i, ncc=ncc, kp=kp: t.matmul(
                            ps_w[:, ncc * 512:(ncc + 1) * 512],
                            sb_xlnT[:, kp * S + i * 128:kp * S + (i + 1) * 128],
                            sb_wv[:, kp * D + ncc * 512:
                                  kp * D + (ncc + 1) * 512],
                            start=(kp == 0), stop=False),
                            deps=[(POOL, cwv)] + (wd if kp == 0 and ncc == 0
                                                  else []))
                    mm = e(PE, lambda t, ncc=ncc: t.matmul(
                        ps_w[:, ncc * 512:(ncc + 1) * 512],
                        c_onesrb[0:1, :],
                        sb_bv[:, ncc * 512:(ncc + 1) * 512],
                        start=False, stop=True))
                vb = i % 2
                ev = e(ACT, lambda a, vb=vb: a.copy(
                    sb_vt[:, vb * NH * 65:(vb + 1) * NH * 65].rearrange(
                        "p (h c) -> p h c", h=NH)[:, :, 0:64],
                    ps_w[:, :].rearrange("p (h c) -> p h c", h=NH)),
                    deps=[(PE, mm), (POOL, conep),
                          (SP, vstores[i - 2] if i >= 2 else 0)])
                rd(("w",), ACT, ev)
                st = 0
                for hh2 in range(NH):
                    st = e(SP, lambda s_, i=i, vb=vb, hh2=hh2: s_.dma_start(
                        out=d_v[hh2, i * 128:(i + 1) * 128, :],
                        in_=sb_vt[:, vb * NH * 65 + hh2 * 65:
                                  vb * NH * 65 + (hh2 + 1) * 65]),
                        deps=[(ACT, ev)], inc=16)
                vstores[i] = st
            sch.barrier()

            c1 = e(ACT, lambda a: a.activation(sb_qrp[0:HD, 0:511],
                                               sb_relext[0:HD, :], AF.Square))
            wd = wr_deps(("rot", 0))
            c2 = e(PE, lambda t: t.matmul(ps_rot[0][0:1, 0:511],
                                          c_onescb[0:HD, 0:1],
                                          sb_qrp[0:HD, 0:511],
                                          start=True, stop=True),
                   deps=[(ACT, c1)] + wd)
            c3 = e(DVE, lambda v: v.reduce_max(sb_row[0:1, RM:RM + 1],
                                               ps_rot[0][0:1, 0:511],
                                               axis=AX.X), deps=[(PE, c2)])
            rd(("rot", 0), DVE, c3)
            relmax_c = e(ACT, lambda a: a.activation(
                sb_row[0:1, RM + 1:RM + 2], sb_row[0:1, RM:RM + 1], AF.Sqrt),
                deps=[(DVE, c3)])

            sqh_pe = [0, 0]
            pend_norm = [None]
            row_dve = [0]
            b1_prev = [0]
            bc_prev = [0]
            mu_prev = [0]
            avst_prev = [0, 0]
            qd_pe = [0]
            for npp in range(NPAIR):
                pe_at_pair = sch.count[PE]
                cwq = e(POOL, lambda g, npp=npp: g.dma_start(
                    out=sb_wv[:, 0:DTILES * 128].rearrange(
                        "p (t c) -> p t c", t=DTILES),
                    in_=ap_wqT[:, npp * 128:(npp + 1) * 128].rearrange(
                        "(t p) c -> p t c", p=128)),
                    deps=[(PE, pe_at_pair)], inc=16)
                cwk = e(POOL, lambda g, npp=npp: g.dma_start(
                    out=sb_wv[:, DTILES * 128:2 * DTILES * 128].rearrange(
                        "p (t c) -> p t c", t=DTILES),
                    in_=ap_wkT[:, npp * 128:(npp + 1) * 128].rearrange(
                        "(t p) c -> p t c", p=128)), inc=16)
                qk_ev = 0
                for wofs, bias_sb, scale, dst, cdep in (
                        (0, sb_bqs, ISQ, sb_qta, cwq),
                        (DTILES * 128, sb_bk2, 1.0, sb_kta, cwk)):
                    for ncc in range(QCH):
                        r = ncc % 2
                        wd = wr_deps(("rot", r))
                        mm = 0
                        for kp in range(DTILES):
                            mm = e(PE, lambda t, ncc=ncc, kp=kp, wofs=wofs,
                                   r=r: t.matmul(
                                       ps_rot[r][:, :],
                                       sb_wv[:, wofs + kp * 128:
                                             wofs + (kp + 1) * 128],
                                       sb_xlnT[:, kp * S + ncc * 512:
                                               kp * S + (ncc + 1) * 512],
                                       start=(kp == 0),
                                       stop=(kp == DTILES - 1)),
                                   deps=[(POOL, cdep), (PE, pe_at_pair)] +
                                   (wd if kp == 0 else []))
                        qk_ev = e(DVE, lambda v, ncc=ncc, dst=dst, r=r,
                                  bias_sb=bias_sb, scale=scale, npp=npp:
                                  v.tensor_scalar(
                                      dst[:, ncc * 512:(ncc + 1) * 512],
                                      ps_rot[r][:, :], scale,
                                      bias_sb[:, npp:npp + 1],
                                      ALU.mult, ALU.add),
                                  deps=[(PE, mm)])
                        rd(("rot", r), DVE, qk_ev)

                RL = RM + 8
                mxl = 0
                for which, src in ((0, sb_kta), (1, sb_qta)):
                    for ncc in range(QCH):
                        sl = (which * QCH + ncc) % 2
                        csq = e(DVE, lambda v, ncc=ncc, src=src, sl=sl:
                                v.tensor_tensor(
                                    sb_sqh[:, sl * 512:(sl + 1) * 512],
                                    src[:, ncc * 512:(ncc + 1) * 512],
                                    src[:, ncc * 512:(ncc + 1) * 512],
                                    ALU.mult),
                                deps=[(DVE, qk_ev), (PE, sqh_pe[sl])])
                        for hh in range(2):
                            r = 2 + hh
                            wd = wr_deps(("rot", r))
                            m1 = e(PE, lambda t, hh=hh, r=r, sl=sl: t.matmul(
                                ps_rot[r][0:1, :],
                                c_onescb[hh * 64:hh * 64 + 64, 0:1],
                                sb_sqh[hh * 64:hh * 64 + 64,
                                       sl * 512:(sl + 1) * 512],
                                start=True, stop=True),
                                deps=[(DVE, csq)] + wd)
                            sqh_pe[sl] = m1
                            m2 = e(DVE, lambda v, which=which, hh=hh,
                                   ncc=ncc, r=r: v.reduce_max(
                                       sb_row[0:1, RL + which * 16 + hh * 8 +
                                              ncc:RL + which * 16 + hh * 8 +
                                              ncc + 1],
                                       ps_rot[r][0:1, :], axis=AX.X),
                                   deps=[(PE, m1)])
                            rd(("rot", r), DVE, m2)
                for hh in range(2):
                    e(DVE, lambda v, hh=hh: v.reduce_max(
                        sb_row[0:1, RL + 32 + hh:RL + 33 + hh],
                        sb_row[0:1, RL + hh * 8:RL + hh * 8 + 8], axis=AX.X))
                    e(DVE, lambda v, hh=hh: v.reduce_max(
                        sb_row[0:1, RL + 34 + hh:RL + 35 + hh],
                        sb_row[0:1, RL + 16 + hh * 8:RL + 16 + hh * 8 + 8],
                        axis=AX.X))
                sq1 = e(ACT, lambda a: a.activation(
                    sb_row[0:1, RL + 36:RL + 40],
                    sb_row[0:1, RL + 32:RL + 36], AF.Sqrt),
                    deps=[(DVE, sch.count[DVE])])
                m5 = e(DVE, lambda v: v.tensor_scalar(
                    sb_row[0:1, RL + 40:RL + 42],
                    sb_row[0:1, RL + 36:RL + 38],
                    sb_row[0:1, RM + 1:RM + 2], None, ALU.add),
                    deps=[(ACT, sq1), (ACT, relmax_c)])
                m6 = e(DVE, lambda v: v.tensor_tensor(
                    sb_row[0:1, RL + 42:RL + 44],
                    sb_row[0:1, RL + 40:RL + 42],
                    sb_row[0:1, RL + 38:RL + 40], ALU.mult))
                m7 = e(DVE, lambda v: v.reduce_max(
                    sb_row[0:1, RL + 44:RL + 45],
                    sb_row[0:1, RL + 42:RL + 44], axis=AX.X))
                wd = wr_deps(("w",))
                m8 = e(PE, lambda t: t.matmul(
                    ps_w[:, 0:1], c_onesr[0:1, :],
                    sb_row[0:1, RL + 44:RL + 45], start=True, stop=True),
                    deps=[(DVE, m7)] + wd)
                m9 = e(DVE, lambda v: v.tensor_scalar(
                    sb_negc[:, :], ps_w[:, 0:1], -1.0, -1.0,
                    ALU.mult, ALU.add),
                    deps=[(PE, m8), (ACT, sch.count[ACT])])
                rd(("w",), DVE, m9)

                kl = e(DVE, lambda v: v.tensor_scalar(
                    sb_klo[:, :], sb_kta[:, :],
                    sb_reledge[:, 0:1], None, ALU.add),
                    deps=[(DVE, qk_ev), (PE, pe_at_pair)])
                kh = e(DVE, lambda v: v.tensor_scalar(
                    sb_khi[:, :], sb_kta[:, :],
                    sb_reledge[:, 1:2], None, ALU.add))

                qrp_sp = [0, 0]
                qspd = {0: 0, 1: 0, 2: 0, 3: 0}
                for hh in range(2):
                    hidx = 2 * npp + hh
                    for qt in range(STILES):
                        r = (hh * STILES + qt) % 4
                        wd = wr_deps(("rot", r))
                        m1 = e(PE, lambda t, hh=hh, qt=qt, r=r: t.matmul(
                            ps_rot[r][:, 0:511],
                            sb_qta[hh * 64:hh * 64 + 64,
                                   qt * 128:(qt + 1) * 128],
                            sb_relext[hh * 64:hh * 64 + 64, :],
                            start=True, stop=True),
                            deps=[(DVE, qk_ev)] + wd)
                        m2 = e(DVE, lambda v, r=r: v.tensor_copy(
                            sb_qrp[:, r * 511:(r + 1) * 511],
                            ps_rot[r][:, 0:511]),
                            deps=[(PE, m1), (SP, qspd[r])])
                        rd(("rot", r), DVE, m2)
                        qspd[r] = e(SP, lambda s_, hidx=hidx, qt=qt, r=r:
                                    s_.dma_start(
                                        out=d_qrp[hidx,
                                                  qt * 128:(qt + 1) * 128,
                                                  :],
                                        in_=sb_qrp[:, r * 511:
                                                   (r + 1) * 511]),
                                    deps=[(DVE, m2)], inc=16)
                        qrp_sp[hh] = qspd[r]

                vh = 0
                for hh in range(2):
                    hidx = 2 * npp + hh
                    vh = e(SP, lambda s_, hidx=hidx, hh=hh: s_.dma_start(
                        out=sb_vh[:, hh * STILES * 65:(hh + 1) * STILES * 65]
                        .rearrange("p (t c) -> p t c", t=STILES),
                        in_=d_v[hidx].rearrange("(t p) c -> p t c", p=128)),
                        deps=[(PE, pe_at_pair)], inc=16)

                for ncc in range(QCH):
                    q0 = ncc * 512
                    qdl = [0, 0]
                    qdh = [0, 0]
                    for hh in range(2):
                        hidx = 2 * npp + hh
                        base = hidx * (S + 1) * 511
                        qdl[hh] = e(SP, lambda s_, base=base, q0=q0, hh=hh:
                                    s_.dma_start_transpose(
                                        sb_qdlo[:, hh * 512:(hh + 1) * 512],
                                        bass.AP(d_qrp.tensor,
                                                base + q0 * 511,
                                                [[511, 512], [1, 128]])),
                                    deps=[(SP, qrp_sp[hh]),
                                          (PE, qd_pe[0])], inc=16)
                        qdh[hh] = e(SP, lambda s_, base=base, q0=q0, hh=hh:
                                    s_.dma_start_transpose(
                                        sb_qdhi[:, hh * 512:(hh + 1) * 512],
                                        bass.AP(d_qrp.tensor,
                                                base + q0 * 511 + 510,
                                                [[511, 512], [1, 128]])),
                                    inc=16)
                    diag_work = []
                    for kt in range(KTILES):
                        d0 = kt * 128 - ncc * 512
                        if not (-255 < d0 < 639):
                            continue
                        for qs in range(4):
                            sd = d0 - qs * 128
                            if sd in (-128, 0, 128):
                                for hh in range(2):
                                    hidx = 2 * npp + hh
                                    src = bass.AP(
                                        d_qrp.tensor,
                                        hidx * (S + 1) * 511 +
                                        (ncc * 512 + qs * 128) * 511 +
                                        (sd + 255),
                                        [[510, 128], [1, 128]])
                                    diag_work.append(
                                        [kt, qs, hh, src, 0, 0])

                    def prefetch_bias(j, npp=npp):
                        if j >= len(diag_work):
                            return
                        src = diag_work[j][3]
                        hh = diag_work[j][2]
                        prevdep = diag_work[j - 8][5] if j >= 8 else 0
                        cc = e(SP, lambda s_, src=src, j=j:
                               s_.dma_start_transpose(
                                   sb_bias[:, (j % 8) * 128:
                                           (j % 8 + 1) * 128], src),
                               deps=[(SP, qrp_sp[hh]), (DVE, prevdep)],
                               inc=16)
                        diag_work[j][4] = cc

                    for j in range(min(8, len(diag_work))):
                        prefetch_bias(j)
                    bi = 0
                    st_mm = {}
                    exp_c = {}
                    av_c = {}
                    SKEW = 2

                    def emit_exp(kt):
                        p = kt % 2
                        (mm, dvec) = st_mm[kt]
                        ex = e(ACT, lambda a, p=p: a.activation(
                            sb_exp[:, p * 1024:(p + 1) * 1024],
                            ps_st[p][:, :], AF.Exp,
                            bias=sb_negc[:, 0:1]),
                            deps=[(PE, mm), (DVE, dvec), (DVE, m9),
                                  (PE, av_c.get(kt - 2, 0))])
                        rd(("rot", 2 * p), ACT, ex)
                        rd(("rot", 2 * p + 1), ACT, ex)
                        exp_c[kt] = ex

                    def emit_ST(kt, ncc=ncc):
                        nonlocal bi
                        p = kt % 2
                        d0 = kt * 128 - ncc * 512
                        if d0 <= -255:
                            kvar = sb_klo
                        elif d0 >= 639:
                            kvar = sb_khi
                        else:
                            kvar = sb_kta
                        subs = []
                        if -255 < d0 < 639:
                            for qs in range(4):
                                sd = d0 - qs * 128
                                if sd in (-128, 0, 128):
                                    subs.append(("diag", qs))
                                elif sd <= -256:
                                    subs.append(("lo", qs))
                                elif sd >= 256:
                                    subs.append(("hi", qs))
                        nsub = len([u for u in subs if u[0] != "diag"])
                        wd = (wr_deps(("rot", 2 * p)) +
                              wr_deps(("rot", 2 * p + 1)))
                        mm = 0
                        for hh in range(2):
                            mm = e(PE, lambda t, kvar=kvar, kt=kt, p=p,
                                   hh=hh, ncc=ncc, nsub=nsub: t.matmul(
                                       ps_st[p][:, hh * 512:(hh + 1) * 512],
                                       kvar[hh * 64:hh * 64 + 64,
                                            kt * 128:(kt + 1) * 128],
                                       sb_qta[hh * 64:hh * 64 + 64,
                                              ncc * 512:(ncc + 1) * 512],
                                       start=True, stop=(nsub == 0)),
                                   deps=[(DVE, kh)] +
                                   (wd if hh == 0 else []))
                        si = 0
                        for (knd, qs) in subs:
                            if knd == "diag":
                                continue
                            si += 1
                            for hh in range(2):
                                if knd == "lo":
                                    lhs = c_onesrb[0:1, :]
                                    rhs_t = sb_qdlo
                                    rlo = 0
                                else:
                                    lhs = c_onesrb[0:1, :]
                                    rhs_t = sb_qdhi
                                    rlo = 0
                                mm = e(PE, lambda t, p=p, qs=qs, hh=hh,
                                       lhs=lhs, rhs_t=rhs_t, rlo=rlo,
                                       si=si, nsub=nsub: t.matmul(
                                           ps_st[p][:, hh * 512 + qs * 128:
                                                 hh * 512 + (qs + 1) * 128],
                                           lhs,
                                           rhs_t[rlo:rlo + 1,
                                                 hh * 512 + qs * 128:
                                                 hh * 512 + (qs + 1) * 128],
                                           start=False,
                                           stop=(si == nsub)),
                                       deps=[(SP, qdh[1])])
                        dvec = 0
                        for (knd, qs) in subs:
                            if knd != "diag":
                                continue
                            for _h in range(2):
                                ent = diag_work[bi]
                                bslot = bi % 8
                                hh = ent[2]
                                dvec = e(DVE, lambda v, p=p, qs=qs, hh=hh,
                                         bslot=bslot: v.tensor_tensor(
                                             ps_st[p][:, hh * 512 + qs * 128:
                                                   hh * 512 + (qs + 1) * 128],
                                             ps_st[p][:, hh * 512 + qs * 128:
                                                   hh * 512 + (qs + 1) * 128],
                                             sb_bias[:, bslot * 128:
                                                     (bslot + 1) * 128],
                                             ALU.add),
                                         deps=[(SP, ent[4]), (PE, mm)])
                                ent[5] = dvec
                                rd(("rot", 2 * p + hh), DVE, dvec)
                                bi += 1
                                prefetch_bias(bi + 7)
                        st_mm[kt] = (mm, dvec)

                    def emit_AV(kt):
                        p = kt % 2
                        last = 0
                        for hh in range(2):
                            avdeps = [(ACT, exp_c[kt]), (SP, vh)]
                            if kt == 0:
                                avdeps += wr_deps(("av", hh))
                            last = e(PE, lambda t, kt=kt, p=p, hh=hh:
                                     t.matmul(
                                         ps_av[hh][0:65, :],
                                         sb_vh[:, hh * STILES * 65 +
                                               kt * 65:
                                               hh * STILES * 65 +
                                               (kt + 1) * 65],
                                         sb_exp[:, p * 1024 + hh * 512:
                                               p * 1024 + (hh + 1) * 512],
                                         start=(kt == 0),
                                         stop=(kt == KTILES - 1)),
                                     deps=avdeps)
                        av_c[kt] = last

                    for step in range(KTILES + SKEW):
                        if step < KTILES:
                            emit_ST(step)
                        if step == 6 and pend_norm[0] is not None:
                            pend_norm[0]()
                            pend_norm[0] = None
                        if step >= SKEW:
                            emit_AV(step - SKEW)
                        if step < KTILES:
                            emit_exp(step)
                    avc = av_c[KTILES - 1]
                    qd_pe[0] = avc

                    rcs = []
                    evs = []
                    for hh in range(2):
                        rc = e(DVE, lambda v, hh=hh: v.reciprocal(
                            sb_rec[64:65, hh * 512:(hh + 1) * 512],
                            ps_av[hh][64:65, :]),
                            deps=[(PE, avc), (PE, bc_prev[0])])
                        rd(("av", hh), DVE, rc)
                        ev = e(ACT, lambda a, hh=hh: a.copy(
                            sb_avf[:, hh * 512:(hh + 1) * 512],
                            ps_av[hh][0:64, :]),
                            deps=[(PE, avc), (DVE, mu_prev[0])])
                        rd(("av", hh), ACT, ev)
                        rcs.append(rc)
                        evs.append(ev)

                    def mknorm(rcs=rcs, evs=evs, ncc=ncc, npp=npp):
                        wd = wr_deps(("w",))
                        bcl = 0
                        for hh in range(2):
                            bcl = e(PE, lambda t, hh=hh: t.matmul(
                                ps_w[0:64, hh * 512:(hh + 1) * 512],
                                c_ones2[64:65, :],
                                sb_rec[64:65, hh * 512:(hh + 1) * 512],
                                start=True, stop=True),
                                deps=[(DVE, rcs[hh])] + (wd if hh == 0
                                                         else []))
                        bc_prev[0] = bcl
                        mu = 0
                        for hh, dstT in ((0, sb_avTa), (1, sb_avTb)):
                            mu = e(DVE, lambda v, hh=hh, dstT=dstT, ncc=ncc:
                                   v.tensor_tensor(
                                       dstT[:, ncc * 512:(ncc + 1) * 512],
                                       sb_avf[:, hh * 512:(hh + 1) * 512],
                                       ps_w[0:64, hh * 512:(hh + 1) * 512],
                                       ALU.mult),
                                   deps=[(PE, bcl), (ACT, evs[hh]),
                                         (SP, avst_prev[hh])])
                            rd(("w",), DVE, mu)
                        mu_prev[0] = mu

                    pend_norm[0] = mknorm
                if pend_norm[0] is not None:
                    pend_norm[0]()
                    pend_norm[0] = None
                for hh, srcT in ((0, sb_avTa), (1, sb_avTb)):
                    hidx = 2 * npp + hh
                    avst_prev[hh] = e(SP, lambda s_, hidx=hidx, srcT=srcT:
                                      s_.dma_start(
                                          out=d_avT[hidx * 64:
                                                    (hidx + 1) * 64, :],
                                          in_=srcT[:, :]),
                                      deps=[(DVE, mu_prev[0])], inc=16)
            sch.barrier()

        with ExitStack() as p5s:
            def psb(name, shape, dtype):
                return p5s.enter_context(nc.sbuf_tensor(name, shape, dtype))

            sb_wo = psb("sb_wo", [128, DTILES * D], BF16)
            sb_avr = psb("sb_avr", [128, DTILES * S], BF16)
            sb_x1t = psb("sb_x1t", [128, 2 * D], F32)

            cwo = e(POOL, lambda g: g.dma_start(
                out=sb_wo[:, :].rearrange("p (t c) -> p t c", t=DTILES),
                in_=ap_woT.rearrange("(t p) c -> p t c", p=128)), inc=16)
            avl = 0
            for ktt in range(DTILES):
                avl = e(SP, lambda s_, ktt=ktt: s_.dma_start(
                    out=sb_avr[:, ktt * S:(ktt + 1) * S],
                    in_=d_avT[ktt * 128:(ktt + 1) * 128, :]), inc=16)

            x1st = [0] * STILES
            stores2 = [0] * STILES
            readers2 = [0] * STILES
            x1dv = [0] * STILES
            for i in range(STILES):
                xb = i % 3
                xv = sb_xin[:, xb * D:(xb + 1) * D]
                dep_reuse = [(DVE, x1dv[i - 3])] if i >= 3 else []
                cl = e(SP, lambda s_, i=i, xv=xv: s_.dma_start(
                    out=xv, in_=ap_x[i * 128:(i + 1) * 128, :]),
                    deps=dep_reuse, inc=16)
                mm = 0
                wd = wr_deps(("w",))
                for ncc in range(2):
                    for ktt in range(DTILES):
                        mm = e(PE, lambda t, i=i, ncc=ncc, ktt=ktt: t.matmul(
                            ps_w[:, ncc * 512:(ncc + 1) * 512],
                            sb_avr[:, ktt * S + i * 128:
                                   ktt * S + (i + 1) * 128],
                            sb_wo[:, ktt * D + ncc * 512:
                                  ktt * D + (ncc + 1) * 512],
                            start=(ktt == 0), stop=False),
                            deps=[(POOL, cwo), (SP, avl)] +
                            (wd if ncc == 0 and ktt == 0 else []))
                    mm = e(PE, lambda t, ncc=ncc: t.matmul(
                        ps_w[:, ncc * 512:(ncc + 1) * 512], c_onesrb[0:1, :],
                        sb_bo[:, ncc * 512:(ncc + 1) * 512],
                        start=False, stop=True))
                xb1 = i % 2
                x1v = sb_x1t[:, xb1 * D:(xb1 + 1) * D]
                dv = 0
                for ncc in range(2):
                    dvd = [(PE, mm), (SP, cl)]
                    if i >= 2:
                        dvd += [(SP, x1st[i - 2]), (DVE, readers2[i - 2])]
                    dv = e(DVE, lambda v, ncc=ncc, x1v=x1v, xv=xv:
                           v.scalar_tensor_tensor(
                               x1v[:, ncc * 512:(ncc + 1) * 512],
                               ps_w[:, ncc * 512:(ncc + 1) * 512], 1.0,
                               xv[:, ncc * 512:(ncc + 1) * 512],
                               ALU.mult, ALU.add), deps=dvd)
                    rd(("w",), DVE, dv)
                x1dv[i] = dv
                x1st[i] = e(SP, lambda s_, i=i, x1v=x1v: s_.dma_start(
                    out=d_x1[i * 128:(i + 1) * 128, :], in_=x1v),
                    deps=[(DVE, dv)], inc=16)
                ob = i % 2
                outv = sb_xln[:, ob * D:(ob + 1) * D]
                pre = [(DVE, dv)]
                odep = [(SP, stores2[i - 2])] if i >= 2 else []
                c10 = layernorm_tile(i % 2, x1v, outv, pre, out_deps=odep)
                readers2[i] = c10
                stores2[i] = e(SP, lambda s_, i=i, outv=outv: s_.dma_start(
                    out=d_xln2[i * 128:(i + 1) * 128, :], in_=outv),
                    deps=[(DVE, c10)], inc=16)
            sch.barrier()

        with ExitStack() as p6s:
            def qsb(name, shape, dtype):
                return p6s.enter_context(nc.sbuf_tensor(name, shape, dtype))

            sb_x2t = qsb("sb_x2t", [128, 2 * 512], BF16)
            sb_aff = qsb("sb_aff", [8, S], F32)
            sb_msk = qsb("sb_msk", [8, S], F32)
            sb_csum = qsb("sb_csum", [8, S], F32)
            sb_mskj = qsb("sb_mskj", [8, S], BF16)
            sb_c16 = qsb("sb_c16", [8, S], F16)
            sb_bis = qsb("sb_bis", [8, 8], F32)
            sb_cT16 = qsb("sb_cT16", [128, STILES], F16)
            sb_cw = qsb("sb_cw", [128, 4 * STILES], F32)
            sb_A = qsb("sb_A", [128, 4 * 128], F32)
            sb_B = qsb("sb_B", [128, 4 * 8], F32)
            sb_sg = qsb("sb_sg", [128, STILES], F32)
            sb_hmask = qsb("sb_hmask", [E, S], F32)

            chm = e(POOL, lambda g: g.dma_start(out=sb_hmask[:, :],
                                                in_=ap_hmask), inc=16)
            for stt in range(STILES):
                e(DVE, lambda v, stt=stt: v.tensor_scalar(
                    sb_sg[:, stt:stt + 1], c_iota_p[:, :], float(stt * 128),
                    None, ALU.add))
            lgp = [0, 0]
            lg_ev = 0
            for ncc in range(QCH):
                r = ncc % 2
                wd = wr_deps(("rot", r))
                mm = 0
                for kp in range(DTILES):
                    tb = (ncc * DTILES + kp) % 2
                    tc = e(SP, lambda s_, ncc=ncc, kp=kp, tb=tb:
                           s_.dma_start_transpose(
                               sb_x2t[:, tb * 512:(tb + 1) * 512],
                               bass.AP(d_xln2.tensor,
                                       (ncc * 512) * D + kp * 128,
                                       [[D, 512], [1, 128]])),
                           deps=[(PE, lgp[tb])], inc=16)
                    mm = e(PE, lambda t, kp=kp, r=r, tb=tb: t.matmul(
                        ps_rot[r][0:8, :],
                        sb_gwT[:, kp * E:(kp + 1) * E],
                        sb_x2t[:, tb * 512:(tb + 1) * 512],
                        start=(kp == 0), stop=(kp == DTILES - 1)),
                        deps=[(SP, tc)] + (wd if kp == 0 else []))
                    lgp[tb] = mm
                lg_ev = e(ACT, lambda a, ncc=ncc, r=r: a.copy(
                    sb_aff[:, ncc * 512:(ncc + 1) * 512], ps_rot[r][0:8, :]),
                    deps=[(PE, mm)])
                rd(("rot", r), ACT, lg_ev)
            ex8 = e(ACT, lambda a: a.activation(sb_aff[:, :], sb_aff[:, :],
                                                AF.Exp), deps=[(ACT, lg_ev)])
            m4 = 0
            for ncc in range(QCH):
                r = ncc % 2
                wd = wr_deps(("rot", r))
                m1 = e(PE, lambda t, ncc=ncc, r=r: t.matmul(
                    ps_rot[r][0:1, :], c_onesc[0:8, 0:1],
                    sb_aff[:, ncc * 512:(ncc + 1) * 512],
                    start=True, stop=True), deps=[(ACT, ex8)] + wd)
                m2 = e(DVE, lambda v, r=r: v.reciprocal(
                    sb_row[0:1, 0:512], ps_rot[r][0:1, :]), deps=[(PE, m1)])
                rd(("rot", r), DVE, m2)
                r2 = 2 + ncc % 2
                wd = wr_deps(("rot", r2))
                m3 = e(PE, lambda t, r2=r2: t.matmul(
                    ps_rot[r2][0:8, :], c_onesr[0:1, 0:8],
                    sb_row[0:1, 0:512], start=True, stop=True),
                    deps=[(DVE, m2)] + wd)
                m4 = e(DVE, lambda v, ncc=ncc, r2=r2: v.tensor_tensor(
                    sb_aff[:, ncc * 512:(ncc + 1) * 512],
                    sb_aff[:, ncc * 512:(ncc + 1) * 512],
                    ps_rot[r2][0:8, :], ALU.mult), deps=[(PE, m3)])
                rd(("rot", r2), DVE, m4)
            if dbg:
                e(SP, lambda s_: s_.dma_start(out=d_aff0, in_=sb_aff[:, :]),
                  deps=[(DVE, m4)], inc=16)
                sch.barrier()
            e(POOL, lambda g: g.memset(sb_bis[:, 0:1], 0.0), deps=[(DVE, m4)])
            e(POOL, lambda g: g.memset(sb_bis[:, 1:2], 1.0))
            bp = e(POOL, lambda g: g.memset(sb_bis[:, 2:3], 0.5))
            for it in range(NBISECT):
                e(DVE, lambda v: v.tensor_scalar(
                    sb_mskj[:, :], sb_aff[:, :], sb_bis[:, 2:3], 0.0,
                    ALU.is_gt, ALU.add, accum_out=sb_bis[:, 3:4]),
                  deps=[(POOL, bp)])
                e(DVE, lambda v: v.tensor_scalar(
                    sb_bis[:, 4:5], sb_bis[:, 3:4], 1024.0, 1.0,
                    ALU.is_ge, ALU.mult))
                e(DVE, lambda v: v.tensor_tensor(
                    sb_bis[:, 5:6], sb_bis[:, 2:3], sb_bis[:, 0:1],
                    ALU.subtract))
                e(DVE, lambda v: v.scalar_tensor_tensor(
                    sb_bis[:, 0:1], sb_bis[:, 5:6], sb_bis[:, 4:5],
                    sb_bis[:, 0:1], ALU.mult, ALU.add))
                e(DVE, lambda v: v.tensor_tensor(
                    sb_bis[:, 6:7], sb_bis[:, 1:2], sb_bis[:, 2:3],
                    ALU.subtract))
                e(DVE, lambda v: v.scalar_tensor_tensor(
                    sb_bis[:, 1:2], sb_bis[:, 6:7], sb_bis[:, 4:5],
                    sb_bis[:, 2:3], ALU.mult, ALU.add))
                e(DVE, lambda v: v.tensor_tensor(
                    sb_bis[:, 5:6], sb_bis[:, 0:1], sb_bis[:, 1:2], ALU.add))
                e(DVE, lambda v: v.tensor_scalar_mul(
                    sb_bis[:, 2:3], sb_bis[:, 5:6], 0.5))
                pb = e(POOL, lambda g: g.nop(),
                       deps=[(DVE, sch.count[DVE])])
                bp = e(POOL, lambda g: g.nop())
                sch.lastw.pop((DVE, POOL), None)
                if dbg and it < 8:
                    e(SP, lambda s_, it=it: s_.dma_start(
                        out=d_bist[:, it * 8:(it + 1) * 8],
                        in_=sb_bis[:, :]),
                      deps=[(DVE, sch.count[DVE])], inc=16)
            if dbg:
                e(SP, lambda s_: s_.dma_start(out=d_bis, in_=sb_bis[:, :]),
                  deps=[(DVE, sch.count[DVE])], inc=16)
            fa = e(DVE, lambda v: v.tensor_copy(sb_c16[:, :], sb_aff[:, :]))
            fb = e(SP, lambda s_: s_.dma_start(out=d_atmp[:, :],
                                               in_=sb_c16[:, :]),
                   deps=[(DVE, fa)], inc=16)
            e(DVE, lambda v: v.tensor_scalar(
                sb_msk[:, :], sb_aff[:, :], sb_bis[:, 0:1], 1.0,
                ALU.is_gt, ALU.mult))
            e(DVE, lambda v: v.tensor_tensor(sb_msk[:, :], sb_msk[:, :],
                                             sb_hmask[:, :], ALU.mult),
              deps=[(POOL, chm)])
            if dbg:
                e(SP, lambda s_: s_.dma_start(out=d_msk, in_=sb_msk[:, :]),
                  deps=[(DVE, sch.count[DVE])], inc=16)
                sch.barrier()
            e(DVE, lambda v: v.tensor_tensor_scan(
                sb_csum[:, :], sb_msk[:, :], sb_msk[:, :], 0.0,
                ALU.add, ALU.bypass))
            if dbg:
                e(SP, lambda s_: s_.dma_start(out=d_csr, in_=sb_csum[:, :]),
                  deps=[(DVE, sch.count[DVE])], inc=16)
                sch.barrier()
            e(DVE, lambda v: v.tensor_tensor(sb_csum[:, :], sb_csum[:, :],
                                             sb_msk[:, :], ALU.mult))
            e(DVE, lambda v: v.tensor_scalar(
                sb_csum[:, :], sb_csum[:, :], -1.0, float(SEL),
                ALU.add, ALU.min))
            e(DVE, lambda v: v.tensor_scalar(
                sb_msk[:, :], sb_csum[:, :], 0.0, 1.0, ALU.is_lt, ALU.mult))
            e(DVE, lambda v: v.tensor_scalar(
                sb_aff[:, :], sb_csum[:, :], -1.0, float(SEL),
                ALU.mult, ALU.add))
            e(DVE, lambda v: v.tensor_tensor(sb_aff[:, :], sb_aff[:, :],
                                             sb_msk[:, :], ALU.mult))
            e(DVE, lambda v: v.tensor_tensor(sb_csum[:, :], sb_csum[:, :],
                                             sb_aff[:, :], ALU.add))
            f8 = e(DVE, lambda v: v.tensor_copy(sb_c16[:, :], sb_csum[:, :]),
                   deps=[(SP, fb)])
            f9 = e(SP, lambda s_: s_.dma_start(out=d_ctmp[:, :],
                                               in_=sb_c16[:, :]),
                   deps=[(DVE, f8)], inc=16)
            mmG = 0
            for ee in range(E):
                t1 = e(SP, lambda s_, ee=ee: s_.dma_start_transpose(
                    sb_cT16[:, :],
                    bass.AP(d_ctmp.tensor, ee * S, [[128, 32], [1, 128]])),
                    deps=[(SP, f9), (DVE, sch.count[DVE])], inc=16)
                t2 = e(DVE, lambda v, ee=ee: v.tensor_copy(
                    sb_cidx[:, ee * STILES:(ee + 1) * STILES], sb_cT16[:, :]),
                    deps=[(SP, t1)])
                t3 = e(DVE, lambda v: v.tensor_copy(
                    sb_cw[:, 0:STILES], sb_cT16[:, :]))
                t4 = e(SP, lambda s_, ee=ee: s_.dma_start_transpose(
                    sb_cT16[:, :],
                    bass.AP(d_atmp.tensor, ee * S, [[128, 32], [1, 128]])),
                    deps=[(SP, fb), (DVE, t3)], inc=16)
                t5 = e(DVE, lambda v: v.tensor_copy(
                    sb_cw[:, 3 * STILES:4 * STILES], sb_cT16[:, :]),
                    deps=[(SP, t4)])
                e(DVE, lambda v: v.tensor_scalar(
                    sb_cw[:, 2 * STILES:3 * STILES], sb_cw[:, 0:STILES],
                    128.0, 1.0, ALU.is_ge, ALU.mult))
                for kk in range(2, 6):
                    e(DVE, lambda v, kk=kk: v.tensor_scalar(
                        sb_cw[:, STILES:2 * STILES], sb_cw[:, 0:STILES],
                        float(128 * kk), 1.0, ALU.is_ge, ALU.mult))
                    e(DVE, lambda v: v.tensor_tensor(
                        sb_cw[:, 2 * STILES:3 * STILES],
                        sb_cw[:, 2 * STILES:3 * STILES],
                        sb_cw[:, STILES:2 * STILES], ALU.add))
                e(DVE, lambda v: v.scalar_tensor_tensor(
                    sb_cw[:, STILES:2 * STILES],
                    sb_cw[:, 2 * STILES:3 * STILES], -128.0,
                    sb_cw[:, 0:STILES], ALU.mult, ALU.add))
                mm_at = [0] * STILES
                for stt in range(STILES):
                    sl = stt % 2
                    adp = [(PE, mm_at[stt - 2])] if stt >= 2 else []
                    e(DVE, lambda v, stt=stt, sl=sl: v.tensor_scalar(
                        sb_A[:, sl * 256:sl * 256 + 128], c_iota_f[:, :],
                        sb_cw[:, STILES + stt:STILES + stt + 1],
                        None, ALU.is_equal), deps=adp)
                    e(DVE, lambda v, stt=stt, sl=sl: v.tensor_scalar(
                        sb_A[:, sl * 256 + 128:sl * 256 + 256],
                        sb_A[:, sl * 256:sl * 256 + 128],
                        sb_sg[:, stt:stt + 1], None, ALU.mult))
                    e(DVE, lambda v, stt=stt, sl=sl: v.tensor_scalar(
                        sb_B[:, sl * 16:sl * 16 + 5], c_iota_f[:, 0:5],
                        sb_cw[:, 2 * STILES + stt:2 * STILES + stt + 1],
                        None, ALU.is_equal))
                    b2_ = e(DVE, lambda v, stt=stt, sl=sl, ee=ee:
                            v.tensor_scalar(
                                sb_B[:, sl * 16 + 8:sl * 16 + 13],
                                sb_B[:, sl * 16:sl * 16 + 5],
                                sb_cw[:, 3 * STILES + stt:
                                      3 * STILES + stt + 1],
                                None, ALU.mult))
                    w0 = wr_deps(("rot", 0)) if stt == 0 else []
                    w1 = wr_deps(("rot", 1)) if stt == 0 else []
                    e(PE, lambda t, stt=stt, sl=sl: t.matmul(
                        ps_rot[0][:, 0:5],
                        sb_A[:, sl * 256 + 128:sl * 256 + 256],
                        sb_B[:, sl * 16:sl * 16 + 5],
                        start=(stt == 0), stop=(stt == STILES - 1)),
                        deps=[(DVE, b2_)] + w0)
                    mmG = e(PE, lambda t, stt=stt, sl=sl: t.matmul(
                        ps_rot[1][:, 0:5],
                        sb_A[:, sl * 256:sl * 256 + 128],
                        sb_B[:, sl * 16 + 8:sl * 16 + 13],
                        start=(stt == 0), stop=(stt == STILES - 1)),
                        deps=w1)
                    mm_at[stt] = mmG
                g1 = e(DVE, lambda v, ee=ee: v.tensor_copy(
                    sb_sidx[:, ee * SELT:(ee + 1) * SELT],
                    ps_rot[0][:, 0:5]), deps=[(PE, mmG)])
                rd(("rot", 0), DVE, g1)
                g2 = e(DVE, lambda v, ee=ee: v.tensor_copy(
                    sb_gsel[:, ee * SELT:(ee + 1) * SELT],
                    ps_rot[1][:, 0:5]))
                rd(("rot", 1), DVE, g2)
            sch.barrier()

        if notail:
            sch.barrier()
        with ExitStack() as p7s:
            def fsb(name, shape, dtype):
                return p7s.enter_context(nc.sbuf_tensor(name, shape, dtype))

            sb_xg = fsb("sb_xg", [128, 2 * D], BF16)
            sb_xselT = fsb("sb_xselT", [128, DTILES * SEL], BF16)
            sb_wg = fsb("sb_wg", [128, 2 * DTILES * 1024], BF16)
            sb_wu = fsb("sb_wu", [128, 2 * DTILES * 1024], BF16)
            sb_wd = fsb("sb_wd", [128, ITILES * 512], BF16)
            sb_gs = fsb("sb_gs", [128, SEL], BF16)
            sb_HT = fsb("sb_HT", [128, ITILES * SEL], BF16)
            sb_yg = fsb("sb_yg", [128, 2 * 512], F32)

            xsel_tr = [0, 0]
            wslot_pe = [0, 0]
            wd_pe = [0, 0]
            last_scat = [0]
            prev_expert_scat = [0]
            for ee in range(0 if notail else E):
                par = ee % 2
                xg_sp = [0] * SELT
                for tt in range(SELT):
                    gb = tt % 2
                    gdp = [(SP, xg_sp[tt - 2])] if tt >= 2 else []
                    cg = e(POOL, lambda g, ee=ee, tt=tt, gb=gb:
                           g.indirect_dma_start(
                               out=sb_xg[:, gb * D:(gb + 1) * D],
                               out_offset=None, in_=d_xln2,
                               in_offset=IndirectOffsetOnAxis(
                                   sb_sidx[:, ee * SELT + tt:
                                           ee * SELT + tt + 1], 0)),
                           deps=gdp, inc=16)
                    xg_sp[tt] = e(SP, lambda s_, ee=ee, tt=tt, gb=gb, par=par:
                                  s_.dma_start(
                                      out=d_xsel[par, tt * 128:(tt + 1) * 128, :],
                                      in_=sb_xg[:, gb * D:(gb + 1) * D]),
                                  deps=[(POOL, cg), (SP, xsel_tr[par])],
                                  inc=16)
                xt = 0
                for dt in range(DTILES):
                    xt = e(SP, lambda s_, dt=dt, par=par:
                           s_.dma_start_transpose(
                               sb_xselT[:, dt * SEL:(dt + 1) * SEL],
                               bass.AP(d_xsel.tensor,
                                       par * SEL * D + dt * 128,
                                       [[D, SEL], [1, 128]])),
                           deps=[(SP, xg_sp[-1]),
                                 (PE, wslot_pe[0]), (PE, wslot_pe[1])],
                           inc=16)
                xsel_tr[par] = xt
                hdone = 0
                ld_g = [0, 0]
                ld_u = [0, 0]
                for ib in range(IBLK):
                    sl = ib % 2
                    ld_g[sl] = e(POOL, lambda g, ee=ee, ib=ib, sl=sl:
                                 g.dma_start(
                                     out=sb_wg[:, sl * DTILES * 1024:
                                               (sl + 1) * DTILES * 1024]
                                     .rearrange("p (t c) -> p t c", t=DTILES),
                                     in_=ap_wgT[ee, :,
                                                ib * 1024:(ib + 1) * 1024]
                                     .rearrange("(t p) c -> p t c", p=128)),
                                 deps=[(PE, wslot_pe[sl])], inc=16)
                    ld_u[sl] = e(POOL, lambda g, ee=ee, ib=ib, sl=sl:
                                 g.dma_start(
                                     out=sb_wu[:, sl * DTILES * 1024:
                                               (sl + 1) * DTILES * 1024]
                                     .rearrange("p (t c) -> p t c", t=DTILES),
                                     in_=ap_wuT[ee, :,
                                                ib * 1024:(ib + 1) * 1024]
                                     .rearrange("(t p) c -> p t c", p=128)),
                                 inc=16)
                    for itl in range(8):
                        it = ib * 8 + itl
                        wd = wr_deps(("w",))
                        mm = 0
                        for dt in range(DTILES):
                            for (cc0, ccn) in ((0, 512), (512, 128)):
                                mm = e(PE, lambda t, sl=sl, dt=dt, itl=itl,
                                       cc0=cc0, ccn=ccn: t.matmul(
                                           ps_w[:, cc0:cc0 + ccn],
                                           sb_wg[:, sl * DTILES * 1024 +
                                                 dt * 1024 + itl * 128:
                                                 sl * DTILES * 1024 +
                                                 dt * 1024 + (itl + 1) * 128],
                                           sb_xselT[:, dt * SEL + cc0:
                                                    dt * SEL + cc0 + ccn],
                                           start=(dt == 0),
                                           stop=(dt == DTILES - 1)),
                                       deps=[(POOL, ld_g[sl]), (SP, xt)] +
                                       (wd if dt == 0 and cc0 == 0 else []))
                        sg = 0
                        for (cc0, ccn) in ((0, 512), (512, 128)):
                            sg = e(ACT, lambda a, cc0=cc0, ccn=ccn:
                                   a.activation(sb_gs[:, cc0:cc0 + ccn],
                                                ps_w[:, cc0:cc0 + ccn],
                                                AF.Silu),
                                   deps=[(PE, mm), (DVE, hdone)])
                        rd(("w",), ACT, sg)
                        wd = wr_deps(("w",))
                        mm = 0
                        for dt in range(DTILES):
                            for (cc0, ccn) in ((0, 512), (512, 128)):
                                mm = e(PE, lambda t, sl=sl, dt=dt, itl=itl,
                                       cc0=cc0, ccn=ccn: t.matmul(
                                           ps_w[:, cc0:cc0 + ccn],
                                           sb_wu[:, sl * DTILES * 1024 +
                                                 dt * 1024 + itl * 128:
                                                 sl * DTILES * 1024 +
                                                 dt * 1024 + (itl + 1) * 128],
                                           sb_xselT[:, dt * SEL + cc0:
                                                    dt * SEL + cc0 + ccn],
                                           start=(dt == 0),
                                           stop=(dt == DTILES - 1)),
                                       deps=[(POOL, ld_u[sl])] +
                                       (wd if dt == 0 and cc0 == 0 else []))
                        hdone = e(DVE, lambda v, it=it: v.tensor_tensor(
                            sb_HT[:, it * SEL:(it + 1) * SEL], sb_gs[:, :],
                            ps_w[:, 0:SEL], ALU.mult),
                            deps=[(PE, mm), (ACT, sg)])
                        rd(("w",), DVE, hdone)
                        wslot_pe[sl] = mm
                banks = [ps_rot[0], ps_rot[1], ps_rot[2], ps_rot[3], ps_av[0]]
                bkeys = [("rot", 0), ("rot", 1), ("rot", 2), ("rot", 3),
                         ("av", 0)]
                ygst = [0, 0]
                prev_expert_scat[0] = last_scat[0]
                for ncc in range(2):
                    wds = [wr_deps(k) for k in bkeys]
                    mm = 0
                    for ih in range(2):
                        wsl = (ncc * 2 + ih) % 2
                        ldd = e(POOL, lambda g, ee=ee, ncc=ncc, ih=ih,
                                wsl=wsl: g.dma_start(
                                    out=sb_wd[:, wsl * 8192:(wsl + 1) * 8192]
                                    .rearrange("p (t c) -> p t c", t=16),
                                    in_=ap_wdT[ee,
                                               ih * 2048:(ih + 1) * 2048,
                                               ncc * 512:(ncc + 1) * 512]
                                    .rearrange("(t p) c -> p t c", p=128)),
                                deps=[(PE, wd_pe[wsl])], inc=16)
                        for itl in range(16):
                            it = ih * 16 + itl
                            for selt in range(SELT):
                                mm = e(PE, lambda t, it=it, itl=itl, selt=selt,
                                       wsl=wsl: t.matmul(
                                           banks[selt][:, :],
                                           sb_HT[:, it * SEL + selt * 128:
                                                 it * SEL + (selt + 1) * 128],
                                           sb_wd[:, wsl * 8192 + itl * 512:
                                                 wsl * 8192 + (itl + 1) * 512],
                                           start=(it == 0),
                                           stop=(it == ITILES - 1)),
                                       deps=[(POOL, ldd), (DVE, hdone)] +
                                       (wds[selt] if it == 0 else []))
                        wd_pe[wsl] = mm
                    for selt in range(SELT):
                        yb = selt % 2
                        ydp = [(PE, mm), (POOL, ygst[yb])]
                        dv = e(DVE, lambda v, ee=ee, selt=selt, yb=yb:
                               v.tensor_scalar(
                                   sb_yg[:, yb * 512:(yb + 1) * 512],
                                   banks[selt][:, :],
                                   sb_gsel[:, ee * SELT + selt:
                                           ee * SELT + selt + 1],
                                   None, ALU.mult), deps=ydp)
                        rd(bkeys[selt], DVE, dv)
                        sdp = [(DVE, dv)]
                        if selt == 0 and ncc == 0:
                            sdp.append((POOL, prev_expert_scat[0]))
                        ygst[yb] = e(POOL, lambda g, ee=ee, selt=selt, yb=yb,
                                     ncc=ncc: g.indirect_dma_start(
                                         out=d_x1,
                                         out_offset=IndirectOffsetOnAxis(
                                             sb_sidx[:, ee * SELT + selt:
                                                     ee * SELT + selt + 1],
                                             0),
                                         in_=sb_yg[:, yb * 512:
                                                   (yb + 1) * 512],
                                         in_offset=None,
                                         element_offset=ncc * 512,
                                         compute_op=ALU.add),
                                     deps=sdp, inc=16)
                        last_scat[0] = ygst[yb]
            sch.barrier()

        for i in range(0 if notail else 4):
            e(SP, lambda s_, i=i: s_.dma_start(
                out=ap_y[i * 1024:(i + 1) * 1024, :],
                in_=d_x1[i * 1024:(i + 1) * 1024, :]), inc=16)

        if dbg:
            sch.barrier()
            dumps = [
                ("o_xln1", d_xln1, [S, D], BF16),
                ("o_avT", d_avT, [D, S], BF16),
                ("o_x1", d_x1, [S, D], F32),
                ("o_xln2", d_xln2, [S, D], BF16),
                ("o_crow", d_crow, [NH, S], BF16),
                ("o_ctmp", d_ctmp, [E, S], F16),
                ("o_atmp", d_atmp, [E, S], F16),
                ("o_v", d_v, [NH, S, 65], BF16),
                ("o_bis", d_bis, [8, 8], F32),
                ("o_bist", d_bist, [8, 64], F32),
                ("o_aff0", d_aff0, [E, S], F32),
                ("o_msk", d_msk, [E, S], F32),
                ("o_csr", d_csr, [E, S], F32),
            ]
            for nm, t, shp, dt in dumps:
                o = nc.dram_tensor(nm, shp, dt, kind="ExternalOutput").ap()
                e(SP, lambda s_, o=o, t=t: s_.dma_start(out=o, in_=t), inc=16)

        final_sp = sch.count[SP]

        with nc.Block() as block:
            def replay(q, eng):
                for (waits, fn, inc) in sch.ops[q]:
                    for (pq, cnt) in waits:
                        eng.wait_ge(sem[pq], cnt)
                    if fn is None:
                        continue
                    ins = fn(eng)
                    ins.then_inc(sem[q], inc)
                if q == SP:
                    eng.wait_ge(sem[SP], final_sp)

            @block.tensor
            def _(t):
                replay(PE, t)

            @block.scalar
            def _(a):
                replay(ACT, a)

            @block.vector
            def _(v):
                replay(DVE, v)

            @block.gpsimd
            def _(g):
                replay(POOL, g)

            @block.sync
            def _(s):
                replay(SP, s)

    return nc


_NC_CACHE = {}
DEBUG = False
NOTAIL = False
DEBUG_RES = {}


def kernel(**inputs):
    B = 4
    x = np.ascontiguousarray(np.asarray(inputs["L0_attn_x"], dtype=np.float32))
    wq = np.asarray(inputs["L0_attn_Wq"], dtype=np.float32)
    wk = np.asarray(inputs["L0_attn_Wk"], dtype=np.float32)
    wv = np.asarray(inputs["L0_attn_Wv"], dtype=np.float32)
    wo = np.asarray(inputs["L0_attn_Wo"], dtype=np.float32)
    bq = np.asarray(inputs["L0_attn_bq"], dtype=np.float32)
    bk = np.asarray(inputs["L0_attn_bk"], dtype=np.float32)
    bv = np.asarray(inputs["L0_attn_bv"], dtype=np.float32)
    bo = np.asarray(inputs["L0_attn_bo"], dtype=np.float32)
    rel = np.asarray(inputs["L0_attn_rel_emb"], dtype=np.float32)
    gw = np.asarray(inputs["L0_ffn_gate_w"], dtype=np.float32)
    wgate = np.asarray(inputs["L0_ffn_w_gate"], dtype=np.float32)
    wup = np.asarray(inputs["L0_ffn_w_up"], dtype=np.float32)
    wdown = np.asarray(inputs["L0_ffn_w_down"], dtype=np.float32)

    wqT = np.ascontiguousarray(wq.T)
    wkT = np.ascontiguousarray(wk.T)
    wvT = np.ascontiguousarray(wv.T)
    woT = np.ascontiguousarray(wo.T)
    bq2 = np.ascontiguousarray(bq.reshape(NPAIR, 128).T)
    bk2 = np.ascontiguousarray(bk.reshape(NPAIR, 128).T)
    bvr = np.ascontiguousarray(bv.reshape(1, D))
    bor = np.ascontiguousarray(bo.reshape(1, D))
    relext1 = rel[np.clip(np.arange(511) - 127, 0, 256)].T
    relext = np.ascontiguousarray(np.concatenate([relext1, relext1], axis=0))
    reledge1 = rel[[0, 256]].T
    reledge = np.ascontiguousarray(
        np.concatenate([reledge1, reledge1], axis=0))
    gwT = np.ascontiguousarray(gw.T.reshape(DTILES, 128, E))
    wgT = np.ascontiguousarray(wgate.transpose(0, 2, 1))
    wuT = np.ascontiguousarray(wup.transpose(0, 2, 1))
    wdT = np.ascontiguousarray(wdown.transpose(0, 2, 1))

    key = ("nc", DEBUG, NOTAIL)
    if key not in _NC_CACHE:
        _NC_CACHE[key] = build_nc(dbg=DEBUG, notail=NOTAIL)
    nc = _NC_CACHE[key]

    in_maps = []
    for c in range(8):
        b, h = c // 2, c % 2
        hm = np.zeros((E, S), dtype=np.float32)
        hm[:, h * 2048:(h + 1) * 2048] = 1.0
        in_maps.append({
            "x": x[b], "wqT": wqT, "wkT": wkT, "wvT": wvT, "woT": woT,
            "bq2": bq2, "bk2": bk2, "bv": bvr, "bo": bor,
            "relext": relext, "reledge": reledge, "gwT": gwT,
            "wgT": wgT, "wuT": wuT, "wdT": wdT, "hmask": hm,
        })
    res = run_bass_kernel_spmd(nc, in_maps, core_ids=list(range(8)))
    if DEBUG:
        DEBUG_RES.clear()
        DEBUG_RES.update({c: res.results[c] for c in range(8)})
    out = np.empty((B, S, D), dtype=np.float32)
    for b in range(B):
        out[b, 0:2048] = np.asarray(res.results[2 * b]["y"])[0:2048]
        out[b, 2048:] = np.asarray(res.results[2 * b + 1]["y"])[2048:]
    return out
